# version 03 (final)
# baseline (speedup 1.0000x reference)
import numpy as np
import ml_dtypes
from contextlib import ExitStack

import concourse.bass as bass
import concourse.mybir as mybir
from concourse.bass_utils import run_bass_kernel_spmd

F32 = mybir.dt.float32
BF16 = mybir.dt.bfloat16
AF = mybir.ActivationFunctionType
ALU = mybir.AluOpType
AX = mybir.AxisListType
BF = ml_dtypes.bfloat16

R = 8
N = 8192
E = 2048
H = 16
D = 128
TOK = N // R
NCH = E // 128
EPS_LN = 1e-5
EPS_RMS = float(np.finfo(np.float32).eps)
TQ = 0.5 + 2.0 ** -10
C_SM = 20.0
ISQD = float(1.0 / np.sqrt(128.0))



class _Sem:
    __slots__ = ("name", "v")

    def __init__(self, name):
        self.name = name
        self.v = 0

    def now(self):
        return (self, self.v)


import os
SERIAL = bool(int(os.environ.get("KSERIAL", "0")))


class Prog:

    def __init__(self):
        self.steps = []
        self.sems = []
        self.v = self.sem("v")
        self.a = self.sem("a")
        self.g = self.sem("g")
        self.t = self.sem("t")
        self.ld = self.sem("ld")
        self.cl = self.sem("cl")
        self.st = self.sem("st")

    def sem(self, name):
        s = _Sem(name)
        self.sems.append(s)
        return s

    _ESEM = {"vector": "v", "scalar": "a"}

    serial_zone = False

    def do(self, eng, fn, waits=(), inc=None, n=1):
        waits = list(waits)
        es = self._ESEM.get(eng)
        if es is not None:
            own = getattr(self, es)
            waits.append(own.now())
            assert inc is None
            inc, n = own, 1
        if SERIAL or self.serial_zone:
            waits = [s.now() for s in self.sems]
        if inc is not None:
            inc.v += n
        self.steps.append((eng, fn, tuple(waits), inc, n))

    def emit(self, nc, ctx):
        semh = {s.name: ctx.enter_context(nc.semaphore(s.name)) for s in self.sems}
        block = ctx.enter_context(nc.Block())
        steps = self.steps

        def run(eng_name, e):
            seen = {}
            for (en, fn, waits, inc, n) in steps:
                if en != eng_name:
                    continue
                for (s, v) in waits:
                    if v > 0 and v > seen.get(s.name, 0):
                        e.wait_ge(semh[s.name], v)
                        seen[s.name] = v
                inst = fn(e)
                if inc is not None:
                    inst.then_inc(semh[inc.name], n)

        @block.sync
        def _(e):
            run("sync", e)

        @block.gpsimd
        def _(e):
            run("gpsimd", e)

        @block.vector
        def _(e):
            run("vector", e)

        @block.scalar
        def _(e):
            run("scalar", e)

        @block.tensor
        def _(e):
            run("tensor", e)


def _f32(x):
    return np.ascontiguousarray(np.asarray(x, dtype=np.float32))


def _bfc(x):
    return np.ascontiguousarray(x)



def build_pre0():
    nc = bass.Bass("TRN2", target_bir_lowering=False, debug=False)
    x = nc.dram_tensor("x", [TOK, E], F32, kind="ExternalInput").ap()
    wgu = nc.dram_tensor("wgu", [16384 // R, E], F32, kind="ExternalInput").ap()
    wp = nc.dram_tensor("wp", [E // R, 4 * E], F32, kind="ExternalInput").ap()
    cst = nc.dram_tensor("cst", [128, 4], F32, kind="ExternalInput").ap()
    xn = nc.dram_tensor("xn", [TOK, E], BF16, kind="ExternalOutput").ap()
    qgu = nc.dram_tensor("qgu", [16384 // R, E], BF16, kind="ExternalOutput").ap()
    qp = nc.dram_tensor("qp", [E // R, 4 * E], BF16, kind="ExternalOutput").ap()

    p = Prog()
    ld, cl, v, a, st = p.ld, p.cl, p.v, p.a, p.st

    with ExitStack() as ctx:
        sb = lambda name, shape, dt=F32: ctx.enter_context(nc.sbuf_tensor(name, shape, dt))
        xt = [sb(f"xt{i}", [128, E]) for i in range(2)]
        xnb = [sb(f"xnb{i}", [128, E], BF16) for i in range(2)]
        sq = sb("sq", [128, E])
        sm = sb("sm", [128, 16])
        cstb = sb("cstb", [128, 4])
        FQ = 4 * E
        wbf = [sb(f"wbf{i}", [128, FQ], BF16) for i in range(2)]
        qa = sb("qa", [128, FQ])
        qb = sb("qb", [128, FQ])
        qc = sb("qc", [128, FQ])
        wh = sb("wh", [128, FQ], BF16)
        gsc = sb("gsc", [128, 3 * 128])
        scb = sb("scb", [128, 128], BF16)

        p.do("sync", lambda e: e.dma_start(out=cstb[:], in_=cst[:]), inc=ld, n=16)

        for i in range(0 if int(os.environ.get("KSKIP_LN","0")) else 8):
            buf = xt[i % 2]
            ob = xnb[i % 2]
            p.do("sync",
                 lambda e, buf=buf, i=i: e.dma_start(out=buf[:], in_=x[i * 128:(i + 1) * 128, :]),
                 waits=[st.now()] if i >= 2 else [], inc=ld, n=16)
            p.do("vector",
                 lambda e, buf=buf: e.reduce_sum(out=sm[:, 0:1], in_=buf[:], axis=AX.X),
                 waits=[ld.now(), a.now()])
            p.do("scalar",
                 lambda e, buf=buf: e.activation(out=sq[:], in_=buf[:], func=AF.Square),
                 waits=[ld.now(), v.now()])
            p.do("vector", lambda e: e.reduce_sum(out=sm[:, 1:2], in_=sq[:], axis=AX.X),
                 waits=[a.now()])
            p.do("vector", lambda e: e.tensor_scalar(out=sm[:, 2:3], in0=sm[:, 0:1],
                                                     scalar1=1.0 / E, scalar2=None, op0=ALU.mult))
            p.do("vector", lambda e: e.tensor_scalar(out=sm[:, 3:4], in0=sm[:, 1:2],
                                                     scalar1=1.0 / E, scalar2=None, op0=ALU.mult))
            p.do("vector", lambda e: e.tensor_tensor(out=sm[:, 4:5], in0=sm[:, 2:3],
                                                     in1=sm[:, 2:3], op=ALU.mult))
            p.do("vector", lambda e: e.tensor_tensor(out=sm[:, 5:6], in0=sm[:, 3:4],
                                                     in1=sm[:, 4:5], op=ALU.subtract))
            p.do("scalar", lambda e: e.activation(out=sm[:, 6:7], in_=sm[:, 5:6],
                                                  func=AF.Sqrt, bias=cstb[:, 0:1]),
                 waits=[v.now()])
            p.do("vector", lambda e: e.reciprocal(out=sm[:, 7:8], in_=sm[:, 6:7]),
                 waits=[a.now()])
            p.do("vector",
                 lambda e, buf=buf, ob=ob: e.tensor_scalar(
                     out=ob[:], in0=buf[:], scalar1=sm[:, 2:3], scalar2=sm[:, 7:8],
                     op0=ALU.subtract, op1=ALU.mult))
            p.do("sync",
                 lambda e, ob=ob, i=i: e.dma_start(out=xn[i * 128:(i + 1) * 128, :], in_=ob[:]),
                 waits=[v.now()], inc=st, n=16)

        tiles = []
        for i in range(wgu.shape[0] // 128):
            tiles.append((wgu, qgu, i, E))
        for i in range(wp.shape[0] // 128):
            tiles.append((wp, qp, i, 4 * E))
        if int(os.environ.get("KSKIP_Q","0")):
            tiles = []
        for t, (src, dst, i, F) in enumerate(tiles):
            G = F // 64
            buf = wbf[t % 2]
            p.do("gpsimd",
                 lambda e, buf=buf, src=src, i=i, F=F: e.dma_start(
                     out=buf[:, 0:F], in_=src[i * 128:(i + 1) * 128, :]),
                 waits=[v.now()], inc=cl, n=16)
            w3 = lambda buf=buf, F=F, G=G: buf[:, 0:F].rearrange("p (g j) -> p g j", j=64)
            p.do("vector",
                 lambda e, w3=w3, G=G: e.reduce_sum(out=gsc[:, 0:G], in_=w3(), axis=AX.X,
                                                    apply_absolute_value=True),
                 waits=[cl.now(), st.now()])
            p.do("vector", lambda e, G=G: e.tensor_scalar(
                out=scb[:, 0:G], in0=gsc[:, 0:G], scalar1=1.0 / 64, scalar2=1e-8,
                op0=ALU.mult, op1=ALU.max))
            p.do("vector", lambda e, G=G: e.tensor_copy(
                out=gsc[:, 128:128 + G], in_=scb[:, 0:G]))
            p.do("vector", lambda e, G=G: e.tensor_scalar(
                out=gsc[:, 256:256 + G], in0=gsc[:, 128:128 + G],
                scalar1=TQ, scalar2=None, op0=ALU.mult))
            p.do("vector", lambda e, G=G: e.tensor_scalar(
                out=gsc[:, 128:128 + G], in0=gsc[:, 128:128 + G],
                scalar1=0.5, scalar2=None, op0=ALU.mult))
            thrB = lambda G=G, F=F: gsc[:, 256:256 + G].unsqueeze(2).broadcast_to([128, G, 64])
            hscB = lambda G=G, F=F: gsc[:, 128:128 + G].unsqueeze(2).broadcast_to([128, G, 64])
            qa3 = lambda F=F: qa[:, 0:F].rearrange("p (g j) -> p g j", j=64)
            qb3 = lambda F=F: qb[:, 0:F].rearrange("p (g j) -> p g j", j=64)
            qc3 = lambda F=F: qc[:, 0:F].rearrange("p (g j) -> p g j", j=64)
            wh3 = lambda F=F: wh[:, 0:F].rearrange("p (g j) -> p g j", j=64)
            p.do("vector", lambda e, w3=w3, thrB=thrB, qa3=qa3: e.tensor_tensor(
                out=qa3(), in0=w3(), in1=thrB(), op=ALU.subtract), waits=[a.now()])
            p.do("scalar", lambda e, qa3=qa3, qb3=qb3: e.activation(
                out=qb3(), in_=qa3(), func=AF.Sign), waits=[v.now()])
            p.do("vector", lambda e, w3=w3, thrB=thrB, qa3=qa3: e.tensor_tensor(
                out=qa3(), in0=w3(), in1=thrB(), op=ALU.add), waits=[a.now()])
            p.do("scalar", lambda e, qa3=qa3, qc3=qc3: e.activation(
                out=qc3(), in_=qa3(), func=AF.Sign), waits=[v.now()])
            p.do("vector", lambda e, qa3=qa3, qb3=qb3, qc3=qc3: e.tensor_tensor(
                out=qa3(), in0=qb3(), in1=qc3(), op=ALU.add), waits=[a.now()])
            p.do("vector", lambda e, qa3=qa3, hscB=hscB, wh3=wh3: e.tensor_tensor(
                out=wh3(), in0=qa3(), in1=hscB(), op=ALU.mult))
            p.do("sync",
                 lambda e, dst=dst, i=i, F=F: e.dma_start(
                     out=dst[i * 128:(i + 1) * 128, :], in_=wh[:, 0:F]),
                 waits=[v.now()], inc=st, n=16)

        nst = st.v
        p.do("sync", lambda e: e.nop(), waits=[st.now()])
        p.emit(nc, ctx)
    return nc


def _run(nc, in_maps, tag=""):
    if os.environ.get("KTRACE"):
        res = run_bass_kernel_spmd(nc, in_maps, core_ids=list(range(R)), trace=True)
        print(f"[KTRACE] {tag} exec_time_ns={res.exec_time_ns}")
        return res.results
    res = run_bass_kernel_spmd(nc, in_maps, core_ids=list(range(R)))
    return res.results



def build_att(layer):
    nc = bass.Bass("TRN2", target_bir_lowering=False, debug=False)
    xnf = nc.dram_tensor("xnf", [E, N], BF16, kind="ExternalInput").ap()
    wq = nc.dram_tensor("wq", [128, 16 * 768], F32, kind="ExternalInput").ap()
    cosf = nc.dram_tensor("cosf", [64, N], F32, kind="ExternalInput").ap()
    sinf = nc.dram_tensor("sinf", [64, N], F32, kind="ExternalInput").ap()
    msk = nc.dram_tensor("msk", [128, 4 * 512], BF16, kind="ExternalInput").ap()
    cst = nc.dram_tensor("cst", [128, 8], F32, kind="ExternalInput").ap()
    if layer == 0:
        lnp = nc.dram_tensor("lnp", [128, 2], F32, kind="ExternalInput").ap()
    else:
        bia = nc.dram_tensor("bia", [128, 6], F32, kind="ExternalInput").ap()
        bvr = nc.dram_tensor("bvr", [1, 256], F32, kind="ExternalInput").ap()
    o = nc.dram_tensor("o", [256, N], BF16, kind="ExternalOutput").ap()
    dbgk = nc.dram_tensor("dbgk", [128, 2048], BF16, kind="ExternalOutput").ap()

    p = Prog()
    ld, cl, v, a, st, t = p.ld, p.cl, p.v, p.a, p.st, p.t

    with ExitStack() as ctx:
        sb = lambda name, shape, dt=F32: ctx.enter_context(nc.sbuf_tensor(name, shape, dt))
        wqb = sb("wqb", [128, 16 * 768], BF16)
        Kres = sb("Kres", [128, 2 * N], BF16)
        Qres = sb("Qres", [128, 2 * N], BF16)
        Vres = sb("Vres", [128, 64 * 256], BF16)
        xnt = [sb(f"xnt{i}", [128, 512], BF16) for i in range(4)]
        cosb = [sb(f"cosb{i}", [128, 512]) for i in range(2)]
        sinb = [sb(f"sinb{i}", [128, 512]) for i in range(2)]
        qraw = sb("qraw", [128, 1024])
        kraw = sb("kraw", [128, 1024])
        ro = sb("ro", [128, 1024])
        rsA = sb("rsA", [64, 1024])
        rsB = sb("rsB", [64, 1024])
        squ = sb("squ", [128, 1024])
        srow = sb("srow", [1, 1024 * 3])
        mskb = sb("mskb", [128, 4 * 512], BF16)
        cstb = sb("cstb", [128, 8])
        prm = sb("prm", [128, 8])
        ones_col = sb("ones_col", [128, 1])
        ones_row = sb("ones_row", [1, 128])
        ones_bf = sb("ones_bf", [128, 1], BF16)
        zeros_bf = sb("zeros_bf", [128, 1], BF16)
        ones_bf1 = sb("ones_bf1", [1, 128], BF16)
        bvb = sb("bvb", [1, 256], BF16)
        s2buf = sb("s2buf", [128, 1024])
        abuf = [sb(f"abuf{i}", [128, 1024], BF16) for i in range(3)]
        recb = sb("recb", [128, 512])
        flbuf = sb("flbuf", [128, 8])
        obuf = [sb(f"obuf{i}", [128, 512], BF16) for i in range(2)]

        p.do("gpsimd", lambda e: e.dma_start(out=wqb[:], in_=wq[:]), inc=cl, n=16)
        p.do("sync", lambda e: e.dma_start(out=mskb[:], in_=msk[:]), inc=ld, n=16)
        p.do("sync", lambda e: e.dma_start(out=cstb[:], in_=cst[:]), inc=ld, n=16)
        if layer == 0:
            p.do("sync", lambda e: e.dma_start(out=prm[:, 0:2], in_=lnp[:]), inc=ld, n=16)
        else:
            p.do("sync", lambda e: e.dma_start(out=prm[:, 0:6], in_=bia[:]), inc=ld, n=16)
            p.do("gpsimd", lambda e: e.dma_start(out=bvb[:], in_=bvr[:]), inc=cl, n=16)
        p.do("vector", lambda e: e.memset(ones_col[:], 1.0))
        p.do("vector", lambda e: e.memset(ones_row[:], 1.0))
        p.do("vector", lambda e: e.memset(ones_bf[:], 1.0))
        p.do("vector", lambda e: e.memset(zeros_bf[:], 0.0))
        p.do("vector", lambda e: e.memset(ones_bf1[:], 1.0))

        def dr3(e, pt):
            mm = None
            for _ in range(3):
                mm = e.matmul(pt[0:1, 0:1], zeros_bf[:], ones_bf[:],
                              start=False, stop=True, skip_group_check=True)
            return mm

        p.serial_zone = bool(int(os.environ.get("KSER_PROD", "0")))
        with ExitStack() as pctx:
            ps = lambda name, shape: pctx.enter_context(nc.psum_tensor(name, shape, F32))
            psU = ps("psU", [128, 1024])
            psV = ps("psV", [128, 1024])
            psK = ps("psK", [128, 1024])
            psQ = ps("psQ", [128, 1024])

            xnt_free_t = [0, 0, 0, 0]
            qk_free_v = 0
            rest_todo = None

            def emit_rest(tt):
                if int(os.environ.get("KSER_REST", "0")):
                    p.serial_zone = True
                extra = []
                if int(os.environ.get("KSER_REST_T", "1")):
                    extra.append(t.now())
                if int(os.environ.get("KSER_REST_LD", "0")):
                    extra.append(ld.now())
                _orig_do = p.do
                if extra:
                    def do2(eng, fn, waits=(), inc=None, n=1, _o=_orig_do, _x=extra):
                        _o(eng, fn, list(waits) + _x, inc=inc, n=n)
                    p.do = do2
                cosv, sinv = cosb[tt % 2], sinb[tt % 2]
                for (src, res_) in ((qraw, Qres), (kraw, Kres)):
                    lo = src[0:64, :].rearrange("p (a b) -> p a b", b=512)
                    hi = src[64:128, :].rearrange("p (a b) -> p a b", b=512)
                    cL = cosv[0:64, :].unsqueeze(1).broadcast_to([64, 2, 512])
                    sL = sinv[0:64, :].unsqueeze(1).broadcast_to([64, 2, 512])
                    cH = cosv[64:128, :].unsqueeze(1).broadcast_to([64, 2, 512])
                    sH = sinv[64:128, :].unsqueeze(1).broadcast_to([64, 2, 512])
                    rsA3 = rsA[:].rearrange("p (a b) -> p a b", b=512)
                    rsB3 = rsB[:].rearrange("p (a b) -> p a b", b=512)
                    p.do("vector", lambda e, lo=lo, cL=cL, rsA3=rsA3: e.tensor_tensor(
                        out=rsA3, in0=lo, in1=cL, op=ALU.mult))
                    p.do("vector", lambda e, hi=hi, sH=sH, rsB3=rsB3: e.tensor_tensor(
                        out=rsB3, in0=hi, in1=sH, op=ALU.mult))
                    p.do("vector", lambda e: e.tensor_tensor(
                        out=ro[0:64, :], in0=rsA[:], in1=rsB[:], op=ALU.subtract),
                        waits=[a.now()])
                    p.do("vector", lambda e, lo=lo, sL=sL, rsA3=rsA3: e.tensor_tensor(
                        out=rsA3, in0=lo, in1=sL, op=ALU.mult))
                    p.do("vector", lambda e, hi=hi, cH=cH, rsB3=rsB3: e.tensor_tensor(
                        out=rsB3, in0=hi, in1=cH, op=ALU.mult))
                    p.do("vector", lambda e: e.tensor_tensor(
                        out=ro[64:128, :], in0=rsA[:], in1=rsB[:], op=ALU.add))
                    if layer == 0:
                        p.do("tensor", lambda e: e.matmul(psU[0:1, 0:512], ones_col[:], ro[:, 0:512],
                                                          start=True, stop=True),
                             waits=[v.now()], inc=t)
                        p.do("tensor", lambda e: (e.matmul(psU[0:1, 512:1024], ones_col[:], ro[:, 512:1024],
                                                           start=True, stop=True), dr3(e, psU))[-1], inc=t)
                        p.do("scalar", lambda e: e.activation(out=squ[:], in_=ro[:], func=AF.Square),
                             waits=[v.now()])
                        p.do("vector", lambda e: e.tensor_scalar(
                            out=srow[:, 0:1024], in0=psU[0:1, :], scalar1=1.0 / 128, scalar2=None,
                            op0=ALU.mult), waits=[t.now()])
                        p.do("tensor", lambda e: e.matmul(psU[0:1, 0:512], ones_col[:], squ[:, 0:512],
                                                          start=True, stop=True),
                             waits=[a.now(), v.now()], inc=t)
                        p.do("tensor", lambda e: (e.matmul(psU[0:1, 512:1024], ones_col[:], squ[:, 512:1024],
                                                           start=True, stop=True), dr3(e, psU))[-1], inc=t)
                        p.do("vector", lambda e: e.tensor_scalar(
                            out=srow[:, 1024:2048], in0=psU[0:1, :], scalar1=1.0 / 128, scalar2=None,
                            op0=ALU.mult), waits=[t.now()])
                        p.do("vector", lambda e: e.tensor_tensor(
                            out=srow[:, 2048:3072], in0=srow[:, 0:1024], in1=srow[:, 0:1024],
                            op=ALU.mult))
                        p.do("vector", lambda e: e.tensor_tensor(
                            out=srow[:, 1024:2048], in0=srow[:, 1024:2048], in1=srow[:, 2048:3072],
                            op=ALU.subtract))
                        p.do("scalar", lambda e: e.activation(
                            out=srow[:, 2048:3072], in_=srow[:, 1024:2048], func=AF.Ln,
                            bias=cstb[0:1, 0:1]), waits=[v.now()])
                        p.do("scalar", lambda e: e.activation(
                            out=srow[:, 1024:2048], in_=srow[:, 2048:3072], func=AF.Exp,
                            scale=cstb[0:1, 1:2]))
                        p.do("tensor", lambda e: e.matmul(psU[:, 0:512], ones_row[:], srow[:, 0:512],
                                                          start=True, stop=True),
                             waits=[v.now()], inc=t)
                        p.do("tensor", lambda e: (e.matmul(psU[:, 512:1024], ones_row[:], srow[:, 512:1024],
                                                           start=True, stop=True), dr3(e, psU))[-1], inc=t)
                        p.do("vector", lambda e: e.tensor_tensor(
                            out=squ[:], in0=ro[:], in1=psU[:], op=ALU.subtract), waits=[t.now()])
                        p.do("tensor", lambda e: e.matmul(psU[:, 0:512], ones_row[:], srow[:, 1024:1536],
                                                          start=True, stop=True),
                             waits=[v.now(), a.now()], inc=t)
                        p.do("tensor", lambda e: (e.matmul(psU[:, 512:1024], ones_row[:], srow[:, 1536:2048],
                                                           start=True, stop=True), dr3(e, psU))[-1], inc=t)
                        p.do("vector", lambda e: e.tensor_tensor(
                            out=ro[:], in0=squ[:], in1=psU[:], op=ALU.mult), waits=[t.now()])
                        p.do("vector", lambda e, res_=res_, tt=tt: e.tensor_scalar(
                            out=res_[:, 0 * N + tt * 512: 0 * N + tt * 512 + 512],
                            in0=ro[:, 0:512], scalar1=prm[:, 0:1], scalar2=prm[:, 1:2],
                            op0=ALU.mult, op1=ALU.add))
                        p.do("vector", lambda e, res_=res_, tt=tt: e.tensor_scalar(
                            out=res_[:, 1 * N + tt * 512: 1 * N + tt * 512 + 512],
                            in0=ro[:, 512:1024], scalar1=prm[:, 0:1], scalar2=prm[:, 1:2],
                            op0=ALU.mult, op1=ALU.add))
                    else:
                        for hcc in range(2):
                            p.do("vector", lambda e, res_=res_, tt=tt, hcc=hcc: e.tensor_copy(
                                out=res_[:, hcc * N + tt * 512: hcc * N + tt * 512 + 512],
                                in_=ro[:, hcc * 512:(hcc + 1) * 512]))

            for tt in range(16):
                if int(os.environ.get("KSER_REST", "0")):
                    p.serial_zone = False
                if hasattr(p.do, '__name__') and p.do.__name__ == 'do2':
                    p.do = Prog.do.__get__(p)
                T = slice(tt * 512, (tt + 1) * 512)
                for half in range(2):
                    p.do("sync", lambda e, tt=tt, half=half: e.dma_start(
                        out=cosb[tt % 2][half * 64:(half + 1) * 64, :],
                        in_=cosf[:, tt * 512:(tt + 1) * 512]),
                        waits=[v.now()] if tt >= 2 and half == 0 else [], inc=ld, n=16)
                    p.do("sync", lambda e, tt=tt, half=half: e.dma_start(
                        out=sinb[tt % 2][half * 64:(half + 1) * 64, :],
                        in_=sinf[:, tt * 512:(tt + 1) * 512]), inc=ld, n=16)
                grp_pend = None
                for kc in range(16):
                    xb = xnt[kc % 4]
                    p.do("sync",
                         lambda e, xb=xb, kc=kc, T=T: e.dma_start(out=xb[:], in_=xnf[kc * 128:(kc + 1) * 128, T]),
                         waits=[(t, xnt_free_t[kc % 4])], inc=ld, n=16)
                    if grp_pend is not None:
                        gfn, gw, gkc = grp_pend
                        p.do("tensor", gfn, waits=gw + [ld.now()], inc=t)
                        xnt_free_t[gkc % 4] = t.v

                    def mmgrp(e, xb=xb, kc=kc, first=(kc == 0), last=(kc == 15)):
                        for c in range(2):
                            e.matmul(psQ[:, c * 512:(c + 1) * 512],
                                     wqb[:, kc * 768 + c * 128: kc * 768 + (c + 1) * 128],
                                     xb[:], start=first, stop=last)
                        for c in range(2):
                            e.matmul(psK[:, c * 512:(c + 1) * 512],
                                     wqb[:, kc * 768 + 256 + c * 128: kc * 768 + 256 + (c + 1) * 128],
                                     xb[:], start=first, stop=last)
                        mm = None
                        for sub in range(4):
                            mm = e.matmul(psV[:, sub * 256:(sub + 1) * 256],
                                          xb[:, sub * 128:(sub + 1) * 128],
                                          wqb[:, kc * 768 + 512: kc * 768 + 768],
                                          start=(first and sub % 2 == 0), stop=(last and layer == 0),
                                          skip_group_check=True)
                        if last and layer == 1:
                            for sub in range(4):
                                mm = e.matmul(psV[:, sub * 256:(sub + 1) * 256],
                                              ones_bf1[:], bvb[:],
                                              start=False, stop=True, skip_group_check=True)
                        if last:
                            mm = dr3(e, psV)
                        return mm

                    grp_pend = (mmgrp, [cl.now()] + ([(v, qk_free_v)] if kc == 0 else []), kc)
                gfn, gw, gkc = grp_pend
                p.do("tensor", gfn, waits=gw + [ld.now()], inc=t)
                xnt_free_t[gkc % 4] = t.v
                mm_done = t.now()

                if rest_todo is not None:
                    emit_rest(rest_todo)
                rest_todo = tt

                p.do("vector", lambda e, tt=tt: e.tensor_copy(
                    out=Vres[:, tt * 1024:(tt + 1) * 1024], in_=psV[:]), waits=[mm_done])
                if layer == 0:
                    p.do("vector", lambda e: e.tensor_copy(out=qraw[:], in_=psQ[:]))
                    p.do("vector", lambda e: e.tensor_copy(out=kraw[:], in_=psK[:]))
                else:
                    p.do("vector", lambda e: e.tensor_scalar(
                        out=qraw[:, 0:512], in0=psQ[:, 0:512], scalar1=prm[:, 0:1],
                        scalar2=None, op0=ALU.add))
                    p.do("vector", lambda e: e.tensor_scalar(
                        out=qraw[:, 512:1024], in0=psQ[:, 512:1024], scalar1=prm[:, 1:2],
                        scalar2=None, op0=ALU.add))
                    p.do("vector", lambda e: e.tensor_scalar(
                        out=kraw[:, 0:512], in0=psK[:, 0:512], scalar1=prm[:, 2:3],
                        scalar2=None, op0=ALU.add))
                    p.do("vector", lambda e: e.tensor_scalar(
                        out=kraw[:, 512:1024], in0=psK[:, 512:1024], scalar1=prm[:, 3:4],
                        scalar2=None, op0=ALU.add))
                qk_free_v = v.v
            emit_rest(rest_todo)

        p.serial_zone = bool(int(os.environ.get("KSER_ATT", "0")))
        with ExitStack() as actx:
            ps = lambda name, shape: actx.enter_context(nc.psum_tensor(name, shape, F32))
            pss = [ps("pss0", [128, 1024]), ps("pss1", [128, 1024])]
            pso = ps("pso", [128, 512])
            psd = ps("psd", [1, 512])
            psb = ps("psb", [128, 512])

            pss_free_a = [a.now()[1], a.now()[1]]
            abuf_free_t = [t.now()[1]] * 3
            s2_free_v = 0
            first_att = True
            hcorder = [1, 0] if int(os.environ.get("KHCREV", "0")) else [0, 1]
            for hc in hcorder:
                for qi in range(16):
                    npr = 2 * qi + 2
                    qsl = slice(hc * N + qi * 512, hc * N + qi * 512 + 512)
                    oden_epi = [v.now(), a.now()]
                    pend = []
                    for pr in range(npr):
                        par = pr % 2
                        ab = pr % 3
                        b0 = 2 * pr
                        w_ = [(a, pss_free_a[par])]
                        if first_att:
                            w_ += [v.now(), a.now()]
                            first_att = False

                        def smm(e, par=par, b0=b0, hc=hc, qsl=qsl):
                            e.matmul(pss[par][:, 0:512],
                                     Kres[:, hc * N + b0 * 128: hc * N + (b0 + 1) * 128],
                                     Qres[:, qsl], start=True, stop=True)
                            e.matmul(pss[par][:, 512:1024],
                                     Kres[:, hc * N + (b0 + 1) * 128: hc * N + (b0 + 2) * 128],
                                     Qres[:, qsl], start=True, stop=True)
                            return dr3(e, pss[par])

                        p.do("tensor", smm, waits=w_, inc=t)
                        smm_t = t.v
                        if len(pend) >= 2:
                            fn, w2, ab2 = pend.pop(0)
                            p.do("tensor", fn, waits=w2, inc=t)
                            abuf_free_t[ab2] = t.v
                        if layer == 0:
                            p.do("scalar",
                                 lambda e, par=par: e.activation(out=s2buf[:], in_=pss[par][:],
                                                                 func=AF.Square),
                                 waits=[(t, smm_t), (v, s2_free_v)])
                            pss_free_a[par] = a.v
                            p.do("vector",
                                 lambda e, ab=ab: e.tensor_tensor(
                                     out=abuf[ab][:], in0=s2buf[:], in1=s2buf[:], op=ALU.mult),
                                 waits=[a.now(), (t, abuf_free_t[ab])])
                            s2_free_v = v.v
                        else:
                            p.do("scalar",
                                 lambda e, par=par, ab=ab: e.activation(
                                     out=abuf[ab][:], in_=pss[par][:], func=AF.Exp,
                                     scale=cstb[:, 4:5], bias=cstb[:, 3:4]),
                                 waits=[(t, smm_t), (t, abuf_free_t[ab])])
                            pss_free_a[par] = a.v
                        if b0 >= 4 * qi:
                            j0 = b0 - 4 * qi
                            mk = mskb[:, j0 * 512:(j0 + 2) * 512]
                            asl = abuf[ab][:]
                            p.do("vector",
                                 lambda e, asl=asl, mk=mk: e.tensor_tensor(
                                     out=asl, in0=asl, in1=mk, op=ALU.mult),
                                 waits=[a.now()])
                            p.do("vector",
                                 lambda e, ab=ab: e.tensor_copy(out=flbuf[:], in_=abuf[ab][:, 0:8]))
                            aready = [v.now()]
                        else:
                            aready = [a.now()] if layer == 1 else [v.now()]

                        def odengrp(e, ab=ab, b0=b0, hc=hc, first=(pr == 0), last=(pr == npr - 1)):
                            for i2 in range(2):
                                b = b0 + i2
                                e.matmul(pso[:], Vres[:, b * 256 + hc * 128: b * 256 + hc * 128 + 128],
                                         abuf[ab][:, i2 * 512:(i2 + 1) * 512],
                                         start=(first and i2 == 0), stop=(last and i2 == 1))
                                mm = e.matmul(psd[0:1, :], ones_bf[:],
                                              abuf[ab][:, i2 * 512:(i2 + 1) * 512],
                                              start=(first and i2 == 0), stop=(last and i2 == 1))
                            if last:
                                mm = dr3(e, pso)
                            return mm

                        pend.append((odengrp, aready + (oden_epi if pr == 0 else []), ab))
                    while pend:
                        fn, w2, ab2 = pend.pop(0)
                        p.do("tensor", fn, waits=w2, inc=t)
                        abuf_free_t[ab2] = t.v
                    p.do("scalar", lambda e: e.activation(out=srow[:, 0:512], in_=psd[0:1, :],
                                                          func=AF.Ln),
                         waits=[t.now()])
                    p.do("scalar", lambda e: e.activation(out=srow[:, 512:1024], in_=srow[:, 0:512],
                                                          func=AF.Exp, scale=cstb[0:1, 2:3]))
                    p.do("tensor", lambda e: (e.matmul(psb[:], ones_row[:], srow[:, 512:1024],
                                                       start=True, stop=True), dr3(e, psb))[-1],
                         waits=[a.now(), v.now()], inc=t)
                    p.do("vector", lambda e: e.tensor_copy(out=recb[:], in_=psb[:]),
                         waits=[t.now()])
                    p.do("vector",
                         lambda e, qi=qi: e.scalar_tensor_tensor(
                             out=obuf[qi % 2][:], in0=pso[:], scalar=1.0, in1=recb[:],
                             op0=ALU.mult, op1=ALU.mult),
                         waits=[st.now()])
                    p.do("sync",
                         lambda e, hc=hc, qi=qi: e.dma_start(
                             out=o[hc * 128:(hc + 1) * 128, qi * 512:(qi + 1) * 512],
                             in_=obuf[qi % 2][:]),
                         waits=[v.now()], inc=st, n=16)

        p.do("sync", lambda e: e.dma_start(out=dbgk[:, 0:512], in_=Kres[:, 0:512]),
             waits=[t.now(), v.now(), a.now()], inc=st, n=16)
        p.do("sync", lambda e: e.dma_start(out=dbgk[:, 512:1024], in_=Qres[:, 0:512]), inc=st, n=16)
        p.do("sync", lambda e: e.dma_start(out=dbgk[:, 1024:1536], in_=Vres[:, 0:512]), inc=st, n=16)
        p.do("sync", lambda e: e.nop(), waits=[st.now()])
        p.emit(nc, ctx)
    return nc



def build_mid0():
    nc = bass.Bass("TRN2", target_bir_lowering=False, debug=False)
    xf = nc.dram_tensor("xf", [E, TOK], F32, kind="ExternalInput").ap()
    of = nc.dram_tensor("of", [E, TOK], BF16, kind="ExternalInput").ap()
    xnof = nc.dram_tensor("xnof", [E, TOK], BF16, kind="ExternalInput").ap()
    wob = nc.dram_tensor("wob", [4, 128, 16 * 512], F32, kind="ExternalInput").ap()
    rtb = nc.dram_tensor("rtb", [128, 16 * 8], F32, kind="ExternalInput").ap()
    egub = nc.dram_tensor("egub", [8, 4, 128, 16 * 512], F32, kind="ExternalInput").ap()
    ednb = nc.dram_tensor("ednb", [8, 4, 128, 8 * 512], F32, kind="ExternalInput").ap()
    idb = nc.dram_tensor("idb", [128, 128], BF16, kind="ExternalInput").ap()
    selb = nc.dram_tensor("selb", [8, 8 * 128], BF16, kind="ExternalInput").ap()
    cst = nc.dram_tensor("cst", [128, 8], F32, kind="ExternalInput").ap()
    x1 = nc.dram_tensor("x1", [E, TOK], F32, kind="ExternalOutput").ap()
    xn1 = nc.dram_tensor("xn1", [E, TOK], BF16, kind="ExternalOutput").ap()

    p = Prog()
    ld, cl, v, a, st, t = p.ld, p.cl, p.v, p.a, p.st, p.t

    with ExitStack() as ctx:
        sb = lambda name, shape, dt=F32: ctx.enter_context(nc.sbuf_tensor(name, shape, dt))
        xfb = sb("xfb", [128, 16 * TOK])
        ofb = sb("ofb", [128, 16 * TOK], BF16)
        xnofb = sb("xnofb", [128, 16 * TOK], BF16)
        wsb = [sb(f"wsb{i}", [128, 16 * 512], BF16) for i in range(2)]
        rtbb = sb("rtbb", [128, 16 * 8], BF16)
        idbb = sb("idbb", [128, 128], BF16)
        selbb = sb("selbb", [8, 8 * 128], BF16)
        cstb = sb("cstb", [128, 8])
        gfm = sb("gfm", [8, TOK], BF16)
        gBb = sb("gBb", [128, 2 * 512], BF16)
        h2b = sb("h2b", [128, 8 * TOK], BF16)
        sg = sb("sg", [128, 512])
        qt = sb("qt", [128, 512])
        squ = [sb(f"squ{i}", [128, 512]) for i in range(2)]
        xnstg = [sb(f"xnstg{i}", [128, 512], BF16) for i in range(2)]
        srow = sb("srow", [1, 512 * 6])
        sm = sb("sm", [128, 16])
        eb = sb("eb", [128, 8])
        prob = sb("prob", [128, 8])
        tmp8 = sb("tmp8", [128, 8])
        gb8 = sb("gb8", [128, 8], BF16)
        ones_col = sb("ones_col", [128, 1])
        ones_bf1 = sb("ones_bf1", [1, 128], BF16)
        ones_row = sb("ones_row", [1, 128])

        for kc in range(16):
            p.do("sync", lambda e, kc=kc: e.dma_start(
                out=xfb[:, kc * TOK:(kc + 1) * TOK], in_=xf[kc * 128:(kc + 1) * 128, :]), inc=ld, n=16)
            p.do("sync", lambda e, kc=kc: e.dma_start(
                out=ofb[:, kc * TOK:(kc + 1) * TOK], in_=of[kc * 128:(kc + 1) * 128, :]), inc=ld, n=16)
            p.do("sync", lambda e, kc=kc: e.dma_start(
                out=xnofb[:, kc * TOK:(kc + 1) * TOK], in_=xnof[kc * 128:(kc + 1) * 128, :]), inc=ld, n=16)
        p.do("sync", lambda e: e.dma_start(out=idbb[:], in_=idb[:]), inc=ld, n=16)
        p.do("sync", lambda e: e.dma_start(out=selbb[:], in_=selb[:]), inc=ld, n=16)
        p.do("sync", lambda e: e.dma_start(out=cstb[:], in_=cst[:]), inc=ld, n=16)
        p.do("gpsimd", lambda e: e.dma_start(out=rtbb[:], in_=rtb[:]), inc=cl, n=16)
        p.do("vector", lambda e: e.memset(ones_col[:], 1.0))
        p.do("vector", lambda e: e.memset(ones_bf1[:], 1.0))
        p.do("vector", lambda e: e.memset(ones_row[:], 1.0))
        ld_all = ld.now()

        with ExitStack() as pctx:
            ps = lambda name, shape, dt=F32: pctx.enter_context(nc.psum_tensor(name, shape, dt))
            pso = [[ps(f"pso{i}{j}", [128, 512]) for j in range(2)] for i in range(4)]

            wfree = [0, 0]
            for ocg in range(4):
                wb = wsb[ocg % 2]
                p.do("gpsimd", lambda e, wb=wb, ocg=ocg: e.dma_start(
                    out=wb[:, 0:4096], in_=wob[ocg, :, 0:4096]), waits=[(t, wfree[ocg % 2])], inc=cl, n=16)
                p.do("gpsimd", lambda e, wb=wb, ocg=ocg: e.dma_start(
                    out=wb[:, 4096:8192], in_=wob[ocg, :, 4096:8192]), inc=cl, n=16)

                def mmo(e, wb=wb, ocg=ocg):
                    mm = None
                    for kc in range(16):
                        for oc in range(4):
                            for nt in range(2):
                                mm = e.matmul(
                                    pso[oc][nt][:],
                                    wb[:, kc * 512 + oc * 128: kc * 512 + (oc + 1) * 128],
                                    ofb[:, kc * TOK + nt * 512: kc * TOK + nt * 512 + 512],
                                    start=(kc == 0), stop=(kc == 15))
                    return mm

                p.do("tensor", mmo, waits=[cl.now(), ld_all, v.now()], inc=t)
                wfree[ocg % 2] = t.v
                for oc in range(4):
                    for nt in range(2):
                        oc_g = ocg * 4 + oc
                        p.do("vector", lambda e, oc=oc, nt=nt, oc_g=oc_g: e.tensor_tensor(
                            out=xfb[:, oc_g * TOK + nt * 512: oc_g * TOK + nt * 512 + 512],
                            in0=xfb[:, oc_g * TOK + nt * 512: oc_g * TOK + nt * 512 + 512],
                            in1=pso[oc][nt][:], op=ALU.add), waits=[t.now()])

            psl = pso[0][0]
            pst = pso[1][0]
            for tb in range(8):
                def mml(e, tb=tb):
                    mm = None
                    for kc in range(16):
                        mm = e.matmul(psl[:, 0:8],
                                      xnofb[:, kc * TOK + tb * 128: kc * TOK + tb * 128 + 128],
                                      rtbb[:, kc * 8:(kc + 1) * 8],
                                      start=(kc == 0), stop=(kc == 15))
                    return mm
                p.do("tensor", mml, waits=[v.now(), a.now(), cl.now()], inc=t)
                p.do("vector", lambda e: e.reduce_max(out=sm[:, 0:1], in_=psl[:, 0:8], axis=AX.X),
                     waits=[t.now()])
                p.do("vector", lambda e: e.tensor_scalar(out=sm[:, 1:2], in0=sm[:, 0:1],
                                                         scalar1=-1.0, scalar2=None, op0=ALU.mult))
                p.do("scalar", lambda e: e.activation(out=eb[:], in_=psl[:, 0:8], func=AF.Exp,
                                                      bias=sm[:, 1:2]), waits=[v.now()])
                p.do("vector", lambda e: e.reduce_sum(out=sm[:, 2:3], in_=eb[:], axis=AX.X),
                     waits=[a.now()])
                p.do("vector", lambda e: e.reciprocal(out=sm[:, 3:4], in_=sm[:, 2:3]))
                p.do("vector", lambda e: e.tensor_scalar(out=prob[:], in0=eb[:],
                                                         scalar1=sm[:, 3:4], scalar2=None, op0=ALU.mult))
                p.do("vector", lambda e: e.reduce_max(out=sm[:, 4:5], in_=prob[:], axis=AX.X))
                p.do("vector", lambda e: e.tensor_scalar(out=tmp8[:], in0=prob[:],
                                                         scalar1=sm[:, 4:5], scalar2=None, op0=ALU.is_ge))
                p.do("vector", lambda e: e.scalar_tensor_tensor(out=tmp8[:], in0=tmp8[:],
                                                                scalar=-1e30, in1=prob[:],
                                                                op0=ALU.mult, op1=ALU.add))
                p.do("vector", lambda e: e.reduce_max(out=sm[:, 5:6], in_=tmp8[:], axis=AX.X))
                p.do("vector", lambda e: e.tensor_tensor(out=sm[:, 6:7], in0=sm[:, 4:5],
                                                         in1=sm[:, 5:6], op=ALU.add))
                p.do("vector", lambda e: e.reciprocal(out=sm[:, 7:8], in_=sm[:, 6:7]))
                p.do("vector", lambda e: e.tensor_scalar(out=tmp8[:], in0=prob[:],
                                                         scalar1=sm[:, 5:6], scalar2=None, op0=ALU.is_ge))
                p.do("vector", lambda e: e.tensor_tensor(out=tmp8[:], in0=tmp8[:], in1=prob[:],
                                                         op=ALU.mult))
                p.do("vector", lambda e: e.tensor_scalar(out=gb8[:], in0=tmp8[:],
                                                         scalar1=sm[:, 7:8], scalar2=None, op0=ALU.mult))
                pstb = pst[:].bitcast(BF16)
                p.do("tensor", lambda e, pstb=pstb: e.transpose(pstb[0:8, 0:128], gb8[:], idbb[:]),
                     waits=[v.now()], inc=t)
                p.do("vector", lambda e, tb=tb, pstb=pstb: e.tensor_copy(
                    out=gfm[:, tb * 128:(tb + 1) * 128], in_=pstb[0:8, 0:128]),
                    waits=[t.now()])

            for ex in range(8):
                for nt in range(2):
                    p.do("tensor", lambda e, ex=ex, nt=nt: e.matmul(
                        pso[3][1][:], selbb[:, ex * 128:(ex + 1) * 128],
                        gfm[0:8, nt * 512: nt * 512 + 512],
                        start=True, stop=True), waits=[v.now()], inc=t)
                    p.do("vector", lambda e, nt=nt: e.tensor_copy(
                        out=gBb[:, nt * 512:(nt + 1) * 512], in_=pso[3][1][:]), waits=[t.now()])
                for pg in range(4):
                    wb = wsb[pg % 2]
                    p.do("gpsimd", lambda e, wb=wb, ex=ex, pg=pg: e.dma_start(
                        out=wb[:, 0:4096], in_=egub[ex, pg, :, 0:4096]), waits=[(t, wfree[pg % 2])], inc=cl, n=16)
                    p.do("gpsimd", lambda e, wb=wb, ex=ex, pg=pg: e.dma_start(
                        out=wb[:, 4096:8192], in_=egub[ex, pg, :, 4096:8192]), inc=cl, n=16)

                    def mmgu(e, wb=wb):
                        mm = None
                        for kc in range(16):
                            for j2 in range(2):
                                for half in range(2):
                                    for nt in range(2):
                                        mm = e.matmul(
                                            pso[j2 * 2 + half][nt][:],
                                            wb[:, kc * 512 + j2 * 256 + half * 128:
                                               kc * 512 + j2 * 256 + half * 128 + 128],
                                            xnofb[:, kc * TOK + nt * 512: kc * TOK + nt * 512 + 512],
                                            start=(kc == 0), stop=(kc == 15))
                        return mm
                    p.do("tensor", mmgu, waits=[cl.now(), v.now(), a.now()], inc=t)
                    wfree[pg % 2] = t.v
                    for j2 in range(2):
                        jg = pg * 2 + j2
                        for nt in range(2):
                            p.do("scalar", lambda e, j2=j2, nt=nt: e.activation(
                                out=sg[:], in_=pso[j2 * 2][nt][:], func=AF.Silu),
                                waits=[t.now(), v.now()])
                            p.do("vector", lambda e, j2=j2, nt=nt: e.scalar_tensor_tensor(
                                out=qt[:], in0=pso[j2 * 2 + 1][nt][:], scalar=1.0, in1=sg[:],
                                op0=ALU.mult, op1=ALU.mult), waits=[a.now()])
                            p.do("vector", lambda e, jg=jg, nt=nt: e.tensor_tensor(
                                out=h2b[:, jg * TOK + nt * 512: jg * TOK + nt * 512 + 512],
                                in0=qt[:], in1=gBb[:, nt * 512:(nt + 1) * 512], op=ALU.mult))
                for og in range(4):
                    wb = wsb[og % 2]
                    p.do("gpsimd", lambda e, wb=wb, ex=ex, og=og: e.dma_start(
                        out=wb[:, 0:2048], in_=ednb[ex, og, :, 0:2048]),
                        waits=[(t, wfree[og % 2])], inc=cl, n=16)
                    p.do("gpsimd", lambda e, wb=wb, ex=ex, og=og: e.dma_start(
                        out=wb[:, 2048:4096], in_=ednb[ex, og, :, 2048:4096]), inc=cl, n=16)

                    def mmdn(e, wb=wb):
                        mm = None
                        for kc in range(8):
                            for oc in range(4):
                                for nt in range(2):
                                    mm = e.matmul(
                                        pso[oc][nt][:],
                                        wb[:, kc * 512 + oc * 128: kc * 512 + (oc + 1) * 128],
                                        h2b[:, kc * TOK + nt * 512: kc * TOK + nt * 512 + 512],
                                        start=(kc == 0), stop=(kc == 7))
                        return mm
                    p.do("tensor", mmdn, waits=[cl.now(), v.now()], inc=t)
                    wfree[og % 2] = t.v
                    for oc in range(4):
                        for nt in range(2):
                            oc_g = og * 4 + oc
                            p.do("vector", lambda e, oc=oc, nt=nt, oc_g=oc_g: e.tensor_tensor(
                                out=xfb[:, oc_g * TOK + nt * 512: oc_g * TOK + nt * 512 + 512],
                                in0=xfb[:, oc_g * TOK + nt * 512: oc_g * TOK + nt * 512 + 512],
                                in1=pso[oc][nt][:], op=ALU.add), waits=[t.now()])

            for kc in range(16):
                p.do("sync", lambda e, kc=kc: e.dma_start(
                    out=x1[kc * 128:(kc + 1) * 128, :], in_=xfb[:, kc * TOK:(kc + 1) * TOK]),
                    waits=[v.now()], inc=st, n=16)
            psS = pso[0][0]
            psQ2 = pso[1][0]
            psB1 = pso[2][0]
            psB2 = pso[3][0]
            sq_free = [0, 0]
            for nt in range(2):
                nsl = slice(nt * 512, nt * 512 + 512)
                v_at_nt = v.now()
                for kc in range(16):
                    p.do("tensor", lambda e, kc=kc, nt=nt: e.matmul(
                        psS[0:1, :], ones_col[:], xfb[:, kc * TOK + nt * 512: kc * TOK + nt * 512 + 512],
                        start=(kc == 0), stop=(kc == 15)),
                        waits=[v.now()] if kc == 0 else [], inc=(t if kc == 15 else None))
                    p.do("scalar", lambda e, kc=kc, nt=nt: e.activation(
                        out=squ[kc % 2][:], in_=xfb[:, kc * TOK + nt * 512: kc * TOK + nt * 512 + 512],
                        func=AF.Square), waits=[(t, sq_free[kc % 2]), v_at_nt])
                    p.do("tensor", lambda e, kc=kc: e.matmul(
                        psQ2[0:1, :], ones_col[:], squ[kc % 2][:],
                        start=(kc == 0), stop=(kc == 15)),
                        waits=[a.now()] + ([v.now()] if kc == 0 else []), inc=t)
                    sq_free[kc % 2] = t.v
                p.do("vector", lambda e: e.tensor_scalar(out=srow[:, 0:512], in0=psS[0:1, :],
                                                         scalar1=1.0 / E, scalar2=None, op0=ALU.mult),
                     waits=[t.now()])
                p.do("vector", lambda e: e.tensor_scalar(out=srow[:, 512:1024], in0=psQ2[0:1, :],
                                                         scalar1=1.0 / E, scalar2=None, op0=ALU.mult))
                p.do("vector", lambda e: e.tensor_tensor(out=srow[:, 1024:1536], in0=srow[:, 0:512],
                                                         in1=srow[:, 0:512], op=ALU.mult))
                p.do("vector", lambda e: e.tensor_tensor(out=srow[:, 1536:2048], in0=srow[:, 512:1024],
                                                         in1=srow[:, 1024:1536], op=ALU.subtract))
                p.do("scalar", lambda e: e.activation(out=srow[:, 2048:2560], in_=srow[:, 1536:2048],
                                                      func=AF.Ln, bias=cstb[0:1, 0:1]),
                     waits=[v.now()])
                p.do("scalar", lambda e: e.activation(out=srow[:, 2560:3072], in_=srow[:, 2048:2560],
                                                      func=AF.Exp, scale=cstb[0:1, 1:2]))
                p.do("tensor", lambda e: e.matmul(psB1[:], ones_row[:], srow[:, 0:512],
                                                  start=True, stop=True), waits=[v.now()], inc=t)
                p.do("tensor", lambda e: e.matmul(psB2[:], ones_row[:], srow[:, 2560:3072],
                                                  start=True, stop=True), waits=[a.now()], inc=t)
                for kc in range(16):
                    p.do("vector", lambda e, kc=kc, nt=nt: e.tensor_tensor(
                        out=squ[kc % 2][:], in0=xfb[:, kc * TOK + nt * 512: kc * TOK + nt * 512 + 512],
                        in1=psB1[:], op=ALU.subtract), waits=[t.now(), st.now()])
                    p.do("vector", lambda e, kc=kc: e.tensor_tensor(
                        out=xnstg[kc % 2][:], in0=squ[kc % 2][:], in1=psB2[:], op=ALU.mult))
                    p.do("sync", lambda e, kc=kc, nt=nt: e.dma_start(
                        out=xn1[kc * 128:(kc + 1) * 128, nt * 512: nt * 512 + 512],
                        in_=xnstg[kc % 2][:]), waits=[v.now()], inc=st, n=16)

        p.do("sync", lambda e: e.nop(), waits=[st.now()])
        p.emit(nc, ctx)
    return nc



def build_fin1():
    nc = bass.Bass("TRN2", target_bir_lowering=False, debug=False)
    xf = nc.dram_tensor("xf", [E, TOK], F32, kind="ExternalInput").ap()
    of = nc.dram_tensor("of", [E, TOK], BF16, kind="ExternalInput").ap()
    xnof = nc.dram_tensor("xnof", [E, TOK], BF16, kind="ExternalInput").ap()
    wob = nc.dram_tensor("wob", [4, 128, 16 * 512], F32, kind="ExternalInput").ap()
    boc = nc.dram_tensor("boc", [128, 16], F32, kind="ExternalInput").ap()
    wgub = nc.dram_tensor("wgub", [64, 128, 16 * 256], BF16, kind="ExternalInput").ap()
    wpb = nc.dram_tensor("wpb", [2, 32, 128, 2 * 1024], BF16, kind="ExternalInput").ap()
    cst = nc.dram_tensor("cst", [128, 8], F32, kind="ExternalInput").ap()
    out = nc.dram_tensor("out", [E, TOK], F32, kind="ExternalOutput").ap()

    p = Prog()
    ld, cl, v, a, st, t = p.ld, p.cl, p.v, p.a, p.st, p.t

    with ExitStack() as ctx:
        sb = lambda name, shape, dt=F32: ctx.enter_context(nc.sbuf_tensor(name, shape, dt))
        xfb = sb("xfb", [128, 16 * TOK])
        xnofb = sb("xnofb", [128, 16 * TOK], BF16)
        h2b = sb("h2b", [128, 64 * 512], BF16)
        sg = sb("sg", [128, 512])
        rstdb = sb("rstdb", [128, 512])
        srow = sb("srow", [1, 512 * 2])
        bocb = sb("bocb", [128, 16])
        cstb = sb("cstb", [128, 8])
        ones_col = sb("ones_col", [128, 1])
        ones_row = sb("ones_row", [1, 128])

        for kc in range(16):
            p.do("sync", lambda e, kc=kc: e.dma_start(
                out=xfb[:, kc * TOK:(kc + 1) * TOK], in_=xf[kc * 128:(kc + 1) * 128, :]), inc=ld, n=16)
            p.do("sync", lambda e, kc=kc: e.dma_start(
                out=xnofb[:, kc * TOK:(kc + 1) * TOK], in_=xnof[kc * 128:(kc + 1) * 128, :]), inc=ld, n=16)
        p.do("sync", lambda e: e.dma_start(out=bocb[:], in_=boc[:]), inc=ld, n=16)
        p.do("sync", lambda e: e.dma_start(out=cstb[:], in_=cst[:]), inc=ld, n=16)
        p.do("vector", lambda e: e.memset(ones_col[:], 1.0))
        p.do("vector", lambda e: e.memset(ones_row[:], 1.0))
        ld_all = ld.now()

        with ExitStack() as actx:
            sba = lambda name, shape, dt=F32: actx.enter_context(nc.sbuf_tensor(name, shape, dt))
            wsb = [sba(f"wsb{i}", [128, 16 * 512], BF16) for i in range(2)]
            ofs = [sba(f"ofs{i}", [128, TOK], BF16) for i in range(2)]
            pctx = actx
            ps = lambda name, shape, dt=F32: pctx.enter_context(nc.psum_tensor(name, shape, dt))
            pso = [[ps(f"pso{i}{j}", [128, 512]) for j in range(2)] for i in range(4)]
            wfree = [0, 0]
            offree = [0, 0]
            evac_done = [0, 0]
            pend = None
            for ocg2 in range(8):
                ocg4 = ocg2 // 2
                st_ = ocg2 % 2
                wb = wsb[ocg4 % 2]
                if ocg2 % 2 == 0:
                    p.do("gpsimd", lambda e, wb=wb, ocg4=ocg4: e.dma_start(
                        out=wb[:, 0:4096], in_=wob[ocg4, :, 0:4096]), waits=[(t, wfree[ocg4 % 2])], inc=cl, n=16)
                    p.do("gpsimd", lambda e, wb=wb, ocg4=ocg4: e.dma_start(
                        out=wb[:, 4096:8192], in_=wob[ocg4, :, 4096:8192]), inc=cl, n=16)
                mmo_pend = None
                for kc in range(16):
                    ob = ofs[kc % 2]
                    p.do("sync", lambda e, ob=ob, kc=kc: e.dma_start(
                        out=ob[:], in_=of[kc * 128:(kc + 1) * 128, :]),
                        waits=[(t, offree[kc % 2])], inc=ld, n=16)
                    if mmo_pend is not None:
                        pfn, pw, pkc = mmo_pend
                        p.do("tensor", pfn, waits=pw + [ld.now()], inc=t)
                        offree[pkc % 2] = t.v
                        wfree[ocg4 % 2] = t.v

                    def mmo(e, wb=wb, kc=kc, ocg2=ocg2, st_=st_):
                        mm = None
                        for oc2 in range(2):
                            oc = (ocg2 % 2) * 2 + oc2
                            for nt in range(2):
                                mm = e.matmul(
                                    pso[st_ * 2 + oc2][nt][:],
                                    wb[:, kc * 512 + oc * 128: kc * 512 + (oc + 1) * 128],
                                    ofs[kc % 2][:, nt * 512: nt * 512 + 512],
                                    start=(kc == 0), stop=(kc == 15))
                        return mm

                    mmo_pend = (mmo, [cl.now()] + ([(v, evac_done[st_])] if kc == 0 else []) +
                                ([ld_all] if ocg2 == 0 else []), kc)
                pfn, pw, pkc = mmo_pend
                p.do("tensor", pfn, waits=pw + [ld.now()], inc=t)
                offree[pkc % 2] = t.v
                wfree[ocg4 % 2] = t.v
                mmo_done = t.now()
                if pend is not None:
                    pend()
                def evacs(ocg2=ocg2, st_=st_, mmo_done=mmo_done):
                    for oc2 in range(2):
                        for nt in range(2):
                            oc_g = (ocg2 // 2) * 4 + (ocg2 % 2) * 2 + oc2
                            p.do("vector", lambda e, oc2=oc2, nt=nt, oc_g=oc_g, st_=st_: e.scalar_tensor_tensor(
                                out=xfb[:, oc_g * TOK + nt * 512: oc_g * TOK + nt * 512 + 512],
                                in0=pso[st_ * 2 + oc2][nt][:], scalar=bocb[:, oc_g:oc_g + 1],
                                in1=xfb[:, oc_g * TOK + nt * 512: oc_g * TOK + nt * 512 + 512],
                                op0=ALU.add, op1=ALU.add), waits=[mmo_done])
                    evac_done[st_] = v.v
                pend = evacs
            pend()

        with ExitStack() as fctx:
            sbf = lambda name, shape, dt=F32: fctx.enter_context(nc.sbuf_tensor(name, shape, dt))
            wgs = [sbf(f"wgs{i}", [128, 16 * 256], BF16) for i in range(2)]
            wps = [sbf(f"wps{i}", [128, 2 * 1024], BF16) for i in range(2)]
            sg2 = [sbf(f"sg2_{i}", [128, 512]) for i in range(2)]
            squ2 = [sbf(f"squ2_{i}", [128, 512]) for i in range(2)]
            gfree = [0, 0]
            pfree = [0, 0]
            sqfree = [0, 0]
            sgfree_v = [0, 0]
            silu_done_a = [0, 0]
            h_done_v = [0, 0]
            for nt in range(2):
                with ExitStack() as pctx:
                    ps = lambda name, shape, dt=F32: pctx.enter_context(nc.psum_tensor(name, shape, dt))
                    psG = [ps(f"psG0n{nt}", [128, 512]), ps(f"psG1n{nt}", [128, 512])]
                    psU = [ps(f"psU0n{nt}", [128, 512]), ps(f"psU1n{nt}", [128, 512])]
                    psd = ps(f"psdn{nt}", [1, 512])
                    pendq = None
                    for j in range(64):
                        jp = j % 2
                        wg = wgs[jp]
                        p.do("gpsimd", lambda e, wg=wg, j=j: e.dma_start(
                            out=wg[:, 0:2048], in_=wgub[j, :, 0:2048]), waits=[(t, gfree[jp])], inc=cl, n=16)
                        p.do("gpsimd", lambda e, wg=wg, j=j: e.dma_start(
                            out=wg[:, 2048:4096], in_=wgub[j, :, 2048:4096]), inc=cl, n=16)

                        def mmgu(e, wg=wg, nt=nt, jp=jp):
                            mm = None
                            for kc in range(16):
                                e.matmul(psG[jp][:], wg[:, kc * 256: kc * 256 + 128],
                                         xnofb[:, kc * TOK + nt * 512: kc * TOK + nt * 512 + 512],
                                         start=(kc == 0), stop=(kc == 15))
                                mm = e.matmul(psU[jp][:], wg[:, kc * 256 + 128: kc * 256 + 256],
                                              xnofb[:, kc * TOK + nt * 512: kc * TOK + nt * 512 + 512],
                                              start=(kc == 0), stop=(kc == 15))
                            return mm

                        p.do("tensor", mmgu,
                             waits=[cl.now(), (a, silu_done_a[jp]), (v, h_done_v[jp])] +
                                   ([v.now(), a.now()] if j == 0 else []),
                             inc=t)
                        gfree[jp] = t.v
                        smm = t.v
                        if pendq is not None:
                            pendq()
                        p.do("scalar", lambda e, jp=jp: e.activation(
                            out=sg2[jp][:], in_=psG[jp][:], func=AF.Silu),
                            waits=[(t, smm), (v, sgfree_v[jp])])
                        silu_done_a[jp] = a.v
                        p.do("vector", lambda e, j=j, jp=jp: e.scalar_tensor_tensor(
                            out=h2b[:, j * 512:(j + 1) * 512], in0=psU[jp][:], scalar=1.0,
                            in1=sg2[jp][:], op0=ALU.mult, op1=ALU.mult), waits=[a.now()])
                        h_done_v[jp] = v.v
                        sgfree_v[jp] = v.v
                        p.do("scalar", lambda e, j=j, jp=jp: e.activation(
                            out=squ2[jp][:], in_=h2b[:, j * 512:(j + 1) * 512], func=AF.Square),
                            waits=[v.now(), (t, sqfree[jp])])
                        sq_a = a.v

                        def pq(j=j, jp=jp, sq_a=sq_a):
                            p.do("tensor", lambda e, j=j, jp=jp: e.matmul(
                                psd[0:1, :], ones_col[:], squ2[jp][:],
                                start=(j == 0), stop=(j == 63)),
                                waits=[(a, sq_a)], inc=t)
                            sqfree[jp] = t.v
                        pendq = pq
                    pendq()
                    p.do("scalar", lambda e: e.activation(out=srow[:, 0:512], in_=psd[0:1, :],
                                                          func=AF.Ln, scale=cstb[0:1, 3:4],
                                                          bias=cstb[0:1, 0:1]), waits=[t.now()])
                    p.do("scalar", lambda e: e.activation(out=srow[:, 512:1024], in_=srow[:, 0:512],
                                                          func=AF.Exp, scale=cstb[0:1, 1:2]))
                    p.do("tensor", lambda e: e.matmul(psG[0][:], ones_row[:], srow[:, 512:1024],
                                                      start=True, stop=True),
                         waits=[a.now(), v.now()], inc=t)
                    p.do("vector", lambda e: e.tensor_copy(out=rstdb[:], in_=psG[0][:]), waits=[t.now()])
                with ExitStack() as pctx:
                    ps = lambda name, shape, dt=F32: pctx.enter_context(nc.psum_tensor(name, shape, dt))
                    psD = [ps(f"psD{i}n{nt}", [128, 512]) for i in range(8)]
                    for og in range(2):
                        for kcg in range(32):
                            kp = kcg % 2
                            wpob = wps[kp]
                            p.do("gpsimd", lambda e, og=og, kcg=kcg, wpob=wpob: e.dma_start(
                                out=wpob[:, 0:1024], in_=wpb[og, kcg, :, 0:1024]),
                                waits=[(t, pfree[kp])], inc=cl, n=16)
                            p.do("gpsimd", lambda e, og=og, kcg=kcg, wpob=wpob: e.dma_start(
                                out=wpob[:, 1024:2048], in_=wpb[og, kcg, :, 1024:2048]), inc=cl, n=16)

                            def mmdn(e, og=og, kcg=kcg, wpob=wpob, nt=nt):
                                mm = None
                                for kc2 in range(2):
                                    kc = kcg * 2 + kc2
                                    for oc in range(8):
                                        mm = e.matmul(
                                            psD[oc][:],
                                            wpob[:, kc2 * 1024 + oc * 128: kc2 * 1024 + (oc + 1) * 128],
                                            h2b[:, kc * 512:(kc + 1) * 512],
                                            start=(kcg == 0 and kc2 == 0),
                                            stop=(kcg == 31 and kc2 == 1))
                                return mm

                            p.do("tensor", mmdn,
                                 waits=[cl.now()] + ([v.now()] if kcg == 0 else []), inc=t)
                            pfree[kp] = t.v
                        for oc in range(8):
                            oc_g = og * 8 + oc
                            p.do("vector", lambda e, oc=oc: e.scalar_tensor_tensor(
                                out=sg[:], in0=psD[oc][:], scalar=1.0, in1=rstdb[:],
                                op0=ALU.mult, op1=ALU.mult), waits=[t.now()])
                            p.do("vector", lambda e, oc_g=oc_g, nt=nt: e.tensor_tensor(
                                out=xfb[:, oc_g * TOK + nt * 512: oc_g * TOK + nt * 512 + 512],
                                in0=xfb[:, oc_g * TOK + nt * 512: oc_g * TOK + nt * 512 + 512],
                                in1=sg[:], op=ALU.add))

        for kc in range(16):
            p.do("sync", lambda e, kc=kc: e.dma_start(
                out=out[kc * 128:(kc + 1) * 128, :], in_=xfb[:, kc * TOK:(kc + 1) * TOK]),
                waits=[v.now()], inc=st, n=16)
        p.do("sync", lambda e: e.nop(), waits=[st.now()])
        p.emit(nc, ctx)
    return nc



def _cosf_sinf():
    invf = 1.0 / (10000.0 ** (np.arange(0, 128, 2, dtype=np.float64) / 128.0))
    ang = np.outer(invf, np.arange(N, dtype=np.float64))
    return np.cos(ang).astype(np.float32), np.sin(ang).astype(np.float32)


def _mask_tiles():
    m = np.zeros((128, 4 * 512), BF)
    kk = np.arange(128)[:, None]
    qq = np.arange(512)[None, :]
    for j in range(4):
        m[:, j * 512:(j + 1) * 512] = (kk + 128 * j <= qq).astype(BF)
    return m


def _att_cst():
    c = np.zeros((128, 8), np.float32)
    c[:, 0] = EPS_LN
    c[:, 1] = -0.5
    c[:, 2] = -1.0
    c[:, 3] = -C_SM
    c[:, 4] = ISQD
    return c


def _wq_blocks(qw, kw, vw):
    q4 = qw.reshape(16, 128, 16, 128)
    k4 = kw.reshape(16, 128, 16, 128)
    v4 = vw.reshape(16, 128, 16, 128)
    outs = []
    for c in range(R):
        hA, hB = 2 * c, 2 * c + 1
        w = np.empty((128, 16, 768), np.float32)
        w[:, :, 0:128] = q4[:, :, hA, :].transpose(1, 0, 2)
        w[:, :, 128:256] = q4[:, :, hB, :].transpose(1, 0, 2)
        w[:, :, 256:384] = k4[:, :, hA, :].transpose(1, 0, 2)
        w[:, :, 384:512] = k4[:, :, hB, :].transpose(1, 0, 2)
        w[:, :, 512:640] = v4[:, :, hA, :].transpose(1, 0, 2)
        w[:, :, 640:768] = v4[:, :, hB, :].transpose(1, 0, 2)
        outs.append(np.ascontiguousarray(w.reshape(128, 16 * 768)))
    return outs


def kernel(**inputs):
    f = _f32
    x2d = f(inputs["L0_attn_x"]).reshape(N, E)
    w_qkv = f(inputs["L0_attn_w_qkv"])
    w_out = f(inputs["L0_attn_w_out"])
    ln_scale = f(inputs["L0_attn_ln_scale"])
    ln_bias = f(inputs["L0_attn_ln_bias"])
    router = f(inputs["L0_ffn_router_gate"])
    egu = f(inputs["L0_ffn_expert_gate_up"])
    edn = f(inputs["L0_ffn_expert_down"])
    Wq = f(inputs["L1_attn_Wq"]); bq = f(inputs["L1_attn_bq"])
    Wk = f(inputs["L1_attn_Wk"]); bk = f(inputs["L1_attn_bk"])
    Wv = f(inputs["L1_attn_Wv"]); bv = f(inputs["L1_attn_bv"])
    Wo = f(inputs["L1_attn_Wo"]); bo = f(inputs["L1_attn_bo"])
    wgu_full = f(inputs["L1_ffn_w_gate_up"])
    wp_full = f(inputs["L1_ffn_w_proj"])

    cosf, sinf = _cosf_sinf()
    msk = _mask_tiles()
    cst_att = _att_cst()
    ident = np.eye(128, dtype=BF)
    selb = np.zeros((8, 8 * 128), BF)
    for e_ in range(8):
        selb[e_, e_ * 128:(e_ + 1) * 128] = 1

    cst1 = np.zeros((128, 4), np.float32)
    cst1[:, 0] = EPS_LN
    nc1 = build_pre0()
    im1 = [{"x": np.ascontiguousarray(x2d[c * TOK:(c + 1) * TOK]),
            "wgu": np.ascontiguousarray(wgu_full[c * 2048:(c + 1) * 2048]),
            "wp": np.ascontiguousarray(wp_full[c * 256:(c + 1) * 256]),
            "cst": cst1} for c in range(R)]
    res1 = _run(nc1, im1)
    xn_full = np.concatenate([np.asarray(r["xn"]) for r in res1], 0)
    w_hat_gu = np.concatenate([np.asarray(r["qgu"]) for r in res1], 0)
    w_hat_p = np.concatenate([np.asarray(r["qp"]) for r in res1], 0)

    xnf0 = np.ascontiguousarray(xn_full.T)
    lnp = np.ascontiguousarray(np.stack([ln_scale, ln_bias], 1))
    wq0 = _wq_blocks(w_qkv[:, 0:E], w_qkv[:, E:2 * E], w_qkv[:, 2 * E:3 * E])
    nc2 = build_att(0)
    im2 = [{"xnf": xnf0, "wq": wq0[c], "cosf": cosf, "sinf": sinf,
            "msk": msk, "cst": cst_att, "lnp": lnp} for c in range(R)]
    res2 = _run(nc2, im2)
    o0 = np.concatenate([np.asarray(r["o"]) for r in res2], 0)

    WOB = np.ascontiguousarray(
        w_out.reshape(16, 128, 4, 512).transpose(2, 1, 0, 3).reshape(4, 128, 16 * 512))
    rtb = np.ascontiguousarray(router.T.reshape(16, 128, 8).transpose(1, 0, 2).reshape(128, 16 * 8))
    ge = egu[:, :, 0::2].reshape(8, 16, 128, 8, 128)
    upw = egu[:, :, 1::2].reshape(8, 16, 128, 8, 128)
    st_ = np.stack([ge, upw], axis=4).reshape(8, 16, 128, 4, 2, 2, 128)
    EGUB = np.ascontiguousarray(st_.transpose(0, 3, 2, 1, 4, 5, 6).reshape(8, 4, 128, 16 * 512))
    EDNB = np.ascontiguousarray(
        edn.reshape(8, 8, 128, 4, 4, 128).transpose(0, 3, 2, 1, 4, 5).reshape(8, 4, 128, 8 * 512))
    cst3 = np.zeros((128, 8), np.float32)
    cst3[:, 0] = EPS_LN; cst3[:, 1] = -0.5; cst3[:, 2] = -1.0
    xfT = np.ascontiguousarray(x2d.T)
    nc3 = build_mid0()
    im3 = []
    for c in range(R):
        tsl = slice(c * TOK, (c + 1) * TOK)
        im3.append({"xf": np.ascontiguousarray(xfT[:, tsl]),
                    "of": np.ascontiguousarray(o0[:, tsl]),
                    "xnof": np.ascontiguousarray(xnf0[:, tsl]),
                    "wob": WOB, "rtb": rtb, "egub": EGUB, "ednb": EDNB,
                    "idb": ident, "selb": selb, "cst": cst3})
    res3 = _run(nc3, im3)
    x1f = np.concatenate([np.asarray(r["x1"]) for r in res3], 1)
    xn1f = np.concatenate([np.asarray(r["xn1"]) for r in res3], 1)

    wq1 = _wq_blocks(np.ascontiguousarray(Wq.T), np.ascontiguousarray(Wk.T),
                     np.ascontiguousarray(Wv.T))
    nc4 = build_att(1)
    im4 = []
    for c in range(R):
        hA, hB = 2 * c, 2 * c + 1
        bia = np.zeros((128, 6), np.float32)
        bia[:, 0] = bq[hA * 128:(hA + 1) * 128]
        bia[:, 1] = bq[hB * 128:(hB + 1) * 128]
        bia[:, 2] = bk[hA * 128:(hA + 1) * 128]
        bia[:, 3] = bk[hB * 128:(hB + 1) * 128]
        bvr = np.concatenate([bv[hA * 128:(hA + 1) * 128],
                              bv[hB * 128:(hB + 1) * 128]])[None, :]
        im4.append({"xnf": np.ascontiguousarray(xn1f), "wq": wq1[c],
                    "cosf": cosf, "sinf": sinf, "msk": msk, "cst": cst_att,
                    "bia": bia, "bvr": np.ascontiguousarray(bvr)})
    res4 = _run(nc4, im4)
    o1 = np.concatenate([np.asarray(r["o"]) for r in res4], 0)

    WoT = np.ascontiguousarray(Wo.T)
    WOB5 = np.ascontiguousarray(
        WoT.reshape(16, 128, 4, 512).transpose(2, 1, 0, 3).reshape(4, 128, 16 * 512))
    boc = np.ascontiguousarray(bo.reshape(16, 128).T)
    A = w_hat_gu.reshape(2, 64, 128, 16, 128)
    WGUB = np.ascontiguousarray(A.transpose(1, 4, 3, 0, 2).reshape(64, 128, 16 * 256))
    B = w_hat_p.reshape(2, 8, 128, 8, 8, 128)
    W1 = B.transpose(0, 3, 5, 4, 1, 2)
    W2 = W1.reshape(2, 8, 128, 4, 2, 8, 128)
    WPB = np.ascontiguousarray(W2.transpose(0, 1, 3, 2, 4, 5, 6).reshape(2, 32, 128, 2 * 1024))
    cst5 = np.zeros((128, 8), np.float32)
    cst5[:, 0] = EPS_RMS; cst5[:, 1] = -0.5; cst5[:, 2] = -1.0
    cst5[:, 3] = 1.0 / 8192.0
    nc5 = build_fin1()
    im5 = []
    for c in range(R):
        tsl = slice(c * TOK, (c + 1) * TOK)
        im5.append({"xf": np.ascontiguousarray(x1f[:, tsl]),
                    "of": np.ascontiguousarray(o1[:, tsl]),
                    "xnof": np.ascontiguousarray(xn1f[:, tsl]),
                    "wob": WOB5, "boc": boc, "wgub": WGUB, "wpb": WPB,
                    "cst": cst5})
    res5 = _run(nc5, im5)
    outf = np.concatenate([np.asarray(r["out"]) for r in res5], 1)
    return np.ascontiguousarray(outf.T).reshape(1, N, E)
